# version 04 (final)
# baseline (speedup 1.0000x reference)
import numpy as np
import ml_dtypes
from contextlib import ExitStack

import concourse.bass as bass
import concourse.mybir as mybir
from concourse.bass_utils import run_bass_kernel_spmd

F32 = mybir.dt.float32
BF16 = mybir.dt.bfloat16
FP8 = mybir.dt.float8e4
ALU = mybir.AluOpType
ACT = mybir.ActivationFunctionType
AX = mybir.AxisListType

D = 2048
S = 8192
NCORES = 8
TPC = S // NCORES
STW = 512
H, KH, HD, W = 16, 4, 128, 512
FFI = 8192
FC = D // 128
DC = D // 128
ICN = FFI // 128
EPS_LN = 1e-5
EPS_RMS = 1e-6
INV448x2 = float(np.float32(2.0) / np.float32(448.0))
SCALE_QK = float(np.float32(1.0) / np.float32(np.sqrt(np.float32(HD))))
NEG_BIG = float(np.finfo(np.float32).min)
NPER = H // KH
DEBUG = 0
SERIAL = 1

DPASS = [(0, 2), (2, 5), (5, 8), (8, 11), (11, 14), (14, 16)]

ENGINES = ("sync", "vector", "scalar", "tensor", "gpsimd")
AOFF = [0, 1, 3, 6, 10, 14, 17, 19]
ACNT = [1, 2, 3, 4, 4, 3, 2, 1]



def _rope_perm(ncols, head_w):
    pp = np.arange(ncols).reshape(ncols // head_w, head_w)
    return np.concatenate([pp[:, 0::2], pp[:, 1::2]], axis=1).reshape(-1)


def _chunk_rearrange(w, pc=128):
    R, C = w.shape
    return np.ascontiguousarray(
        w.reshape(R // pc, pc, C // pc, pc).transpose(2, 1, 0, 3))


def _prep_inputs(x, ln_w, wq, wk, wv, wo, gw, uw, dw):
    xT = np.ascontiguousarray(x[0].T.astype(np.float32))

    wq_r = _chunk_rearrange(np.ascontiguousarray(wq[:, _rope_perm(D, HD)]))
    wk_r = _chunk_rearrange(np.ascontiguousarray(wk[:, _rope_perm(KH * HD, HD)]))
    wv_r = _chunk_rearrange(np.ascontiguousarray(wv))
    wo_r = _chunk_rearrange(np.ascontiguousarray(wo))
    gw_r = np.ascontiguousarray(gw.reshape(ICN, 128, D))
    uw_r = np.ascontiguousarray(uw.reshape(ICN, 128, D))
    dw_r = np.ascontiguousarray(dw.reshape(DC, 128, ICN, 128).transpose(2, 1, 0, 3))
    lnw_r = np.ascontiguousarray(ln_w.reshape(FC, 128).T)

    inv_freq = (np.float32(1.0)
                / (np.float32(10000.0) ** (np.arange(0, HD, 2, dtype=np.float32)
                                           / np.float32(HD))))
    ang = np.arange(S, dtype=np.float32)[:, None] * inv_freq[None, :]
    cosT = np.cos(ang).astype(np.float32).T
    sinT = np.sin(ang).astype(np.float32).T

    qi = np.arange(128)[:, None]
    kj = np.arange(640)[None, :]
    band = (kj >= qi) & (kj <= qi + W)
    base = np.where(band, np.float32(0), np.float32(1)).astype(np.float32)
    masks_norm = np.ascontiguousarray(
        np.broadcast_to(base, (2, 4, 128, 640))).astype(np.float32)
    m0 = masks_norm.copy()
    for a in range(4):
        kglob = (128 * a - 512) + np.arange(640)
        m0[0, a][:, kglob < 0] = np.float32(1)

    identb = np.eye(128, dtype=ml_dtypes.bfloat16)
    identf = np.eye(128, dtype=np.float32)
    onesf = np.ones((128, 128), dtype=np.float32)

    per_core = []
    for c in range(NCORES):
        t0 = c * TPC
        halo = np.zeros((D, STW), np.float32)
        if c > 0:
            halo[:] = xT[:, t0 - STW:t0]
        xh = np.ascontiguousarray(halo.reshape(FC, 128, STW).transpose(1, 0, 2))
        xo = np.ascontiguousarray(
            xT[:, t0:t0 + TPC].reshape(FC, 128, TPC).transpose(1, 0, 2))
        cosc = np.zeros((64, STW + TPC), np.float32)
        sinc = np.zeros((64, STW + TPC), np.float32)
        lo = t0 - STW
        src_lo = max(lo, 0)
        off = src_lo - lo
        cosc[:, off:] = cosT[:, src_lo:t0 + TPC]
        sinc[:, off:] = sinT[:, src_lo:t0 + TPC]
        per_core.append({
            "xh": xh, "xo": xo,
            "wq": wq_r, "wk": wk_r, "wv": wv_r, "wo": wo_r,
            "gw": gw_r, "uw": uw_r, "dw": dw_r,
            "lnw": lnw_r, "cosc": np.ascontiguousarray(cosc),
            "sinc": np.ascontiguousarray(sinc),
            "mask": np.ascontiguousarray(
                (m0 if c == 0 else masks_norm).transpose(0, 2, 1, 3)
                ).astype(ml_dtypes.float8_e4m3),
            "identb": identb, "identf": identf, "onesf": onesf,
        })
    return per_core



class PBuild:
    def __init__(self, nc, es):
        self.nc = nc
        self.es = es
        self.streams = {e: [] for e in ENGINES}
        self.sems = {}
        self.cnt = {}

    def sem(self, name):
        self.sems[name] = self.es.enter_context(self.nc.semaphore(name))
        self.cnt[name] = 0
        return name

    def run(self, eng, fn, inc=None, incn=1, wait=None):
        sems = self.sems
        if SERIAL:
            ser_wait = self.cnt["SER"]
            is_dma = (incn == 16)
            self.cnt["SER"] += 16 if is_dma else 1
            if inc is not None:
                self.cnt[inc] += incn
            val = self.cnt[inc] if inc is not None else None

            def closure(e, fn=fn, ser_wait=ser_wait, d=16 if is_dma else 1):
                if ser_wait > 0:
                    e.wait_ge(sems["SER"], ser_wait)
                fn(e).then_inc(sems["SER"], d)
            self.streams[eng].append(closure)
            return val
        waits = tuple((s, v) for (s, v) in (wait or []) if v and v > 0)
        if inc is not None:
            self.cnt[inc] += incn
        val = self.cnt[inc] if inc is not None else None

        def closure(e, waits=waits, fn=fn, inc=inc, incn=incn):
            for (s, v) in waits:
                e.wait_ge(sems[s], v)
            instr = fn(e)
            if inc is not None:
                instr.then_inc(sems[inc], incn)
        self.streams[eng].append(closure)
        return val

    def wait(self, eng, sem, val):
        sems = self.sems
        if SERIAL:
            v = self.cnt["SER"]
            if v > 0:
                self.streams[eng].append(
                    lambda e, v=v: e.wait_ge(sems["SER"], v))
            return
        if not val or val <= 0:
            return
        self.streams[eng].append(lambda e, s=sem, v=val: e.wait_ge(sems[s], v))

    def v(self, name):
        return self.cnt[name]


def build_program(depth):
    nc = bass.Bass("TRN2", target_bir_lowering=False)
    es = ExitStack()
    p = PBuild(nc, es)

    def din(name, shape, dtype=F32):
        return nc.dram_tensor(name, list(shape), dtype, kind="ExternalInput").ap()

    xh_d = din("xh", [128, FC, STW])
    xo_d = din("xo", [128, FC, TPC])
    wq_d = din("wq", [H, 128, FC, 128])
    wk_d = din("wk", [KH, 128, FC, 128])
    wv_d = din("wv", [KH, 128, FC, 128])
    wo_d = din("wo", [DC, 128, FC, 128])
    gw_d = din("gw", [ICN, 128, D])
    uw_d = din("uw", [ICN, 128, D])
    dw_d = din("dw", [ICN, 128, DC, 128])
    lnw_d = din("lnw", [128, FC])
    cos_d = din("cosc", [64, STW + TPC])
    sin_d = din("sinc", [64, STW + TPC])
    mask_d = din("mask", [2, 128, 4, 640], FP8)
    identb_d = din("identb", [128, 128], BF16)
    identf_d = din("identf", [128, 128])
    onesf_d = din("onesf", [128, 128])

    xout_d = nc.dram_tensor("xout", [128, FC, TPC], F32, kind="ExternalOutput").ap()
    dbg_d = (nc.dram_tensor("dbg", [128, max(DEBUG, 1) * 512], F32,
                            kind="ExternalOutput").ap() if DEBUG else None)
    dbg_i = [0]

    def dbg(ap, extra=None):
        if not DEBUG or dbg_i[0] >= DEBUG:
            return
        i = dbg_i[0]
        dbg_i[0] += 1
        nparts = ap.shape[0]
        n = ap.shape[-1]
        slot = dbg_d[0:nparts, i * 512:(i + 1) * 512]
        if ap.dtype == BF16:
            slot = slot.bitcast(BF16)[:, 0:n]
        elif ap.dtype == FP8:
            slot = slot.bitcast(FP8)[:, 0:n]
        else:
            slot = slot[:, 0:n]
        p.run("sync", lambda e, a=ap, s=slot: e.dma_start(out=s, in_=a),
            inc=p_ST, incn=16,
            wait=[("VC", p.v("VC")), ("SC", p.v("SC")), ("TC", p.v("TC"))]
            + list(extra or []))

    rspill_d = nc.dram_tensor("rspill", [128, FC * TPC], F32, kind="Internal").ap()
    fout_d = nc.dram_tensor("fout", [128, DC * TPC], F32, kind="Internal").ap()
    hsp_d = nc.dram_tensor("hsp", [ICN, 128, 2048], BF16, kind="Internal").ap()

    LD = p.sem("LD")
    ST_ = p.sem("ST")
    p_ST = ST_
    if SERIAL:
        p.sem("SER")
    VC = p.sem("VC")
    SC = p.sem("SC")
    TC = p.sem("TC")

    sbuf = lambda n, shp, dt=F32: es.enter_context(nc.sbuf_tensor(n, list(shp), dt))
    psum = lambda n, shp, dt=F32: es.enter_context(nc.psum_tensor(n, list(shp), dt))

    PA = sbuf("PA", [128, FC * TPC])
    PAb = PA[:].bitcast(BF16)
    PE_ = sbuf("PE", [128, 16384], BF16)
    PB_ = sbuf("PB", [128, 8192], BF16)
    PK_ = sbuf("PK", [128, 16384], BF16)
    PD_ = sbuf("PD", [128, 13312], BF16)
    PW_ = sbuf("PW", [128, 4096])

    resid = PA[:]
    xn_hi = PE_[:, 0:8192]
    xn_lo = PE_[:, 8192:16384]
    ao2_hi = PE_[:, 0:4096]
    ao2_lo = PE_[:, 4096:8192]
    ao1_hi = PD_[:, 0:4096]
    ao1_lo = PD_[:, 4096:8192]

    def ao_hi(h):
        return (ao1_hi if h < 8 else ao2_hi)[:, (h % 8) * STW:(h % 8 + 1) * STW]

    def ao_lo(h):
        return (ao1_lo if h < 8 else ao2_lo)[:, (h % 8) * STW:(h % 8 + 1) * STW]

    q_hi = PB_[:, 0:4096]
    q_lo = PB_[:, 4096:8192]
    attn_out = PB_[:].bitcast(F32)
    k_hi = PK_[:, 0:4096]
    k_lo = PK_[:, 4096:8192]
    v_hi = PK_[:, 8192:12288]
    v_lo = PK_[:, 12288:16384]
    pt_hi = PD_[:, 8192:10752]
    pt_lo = PD_[:, 10752:13312]

    wraw = PW_[:, 0:2048]
    PWb = PW_[:].bitcast(BF16)
    _wwhi = PWb[:, 4096:6144]
    _wwlo = PWb[:, 6144:8192]
    wwhi = [_wwhi, _wwhi]
    wwlo = [_wwlo, _wwlo]

    lnw = sbuf("lnw_sb", [128, FC])
    identb = sbuf("identb_sb", [128, 128], BF16)
    identf = sbuf("identf_sb", [128, 128])
    onesf = sbuf("onesf_sb", [128, 128])
    cs_t = sbuf("cs_t", [64, STW])
    cosb = cs_t[:]
    sn_t = sbuf("sn_t", [64, STW])
    sinb = sn_t[:]
    acc1 = sbuf("acc1", [128, STW])
    acc2 = sbuf("acc2", [128, STW])
    m_t = sbuf("m_t", [128, STW])
    g_t = sbuf("g_t", [128, STW])
    t2_t = sbuf("t2_t", [128, STW])
    sqb = t2_t[:]
    ub = sbuf("ub", [128, STW])
    nmax = sbuf("nmax", [128, 1])
    epsln_t = sbuf("epsln_t", [128, 1])
    epsrms_t = sbuf("epsrms_t", [128, 1])
    sume = sbuf("sume", [128, 1])
    rsum = sbuf("rsum", [128, 1])
    tmp64a = m_t[0:64, :]
    tmp64b = g_t[0:64, :]
    p_hi = acc1[:].bitcast(BF16)[:, 0:640]
    p_lo = acc2[:].bitcast(BF16)[:, 0:640]
    vtmp_hi = acc1[:].bitcast(BF16)[:, 0:512]
    vtmp_lo = acc2[:].bitcast(BF16)[:, 0:512]
    ssb = PW_[:, 0:640]
    esb = PW_[:, 640:1280]
    maskbuf = sbuf("maskbuf", [128, 2560], FP8)[:]
    nmaxs = sbuf("nmaxs", [128, 1])

    def _xn2r(fc, ch):
        return PAb[:, fc * 2048 + ch * 1024: fc * 2048 + (ch + 1) * 1024
                   ].rearrange("p (a b) -> p a b", a=2)

    def xn2hi(fc, ch):
        return _xn2r(fc, ch)[:, :, 0:256]

    def xn2lo(fc, ch):
        return _xn2r(fc, ch)[:, :, 256:512]

    draw = [PB_[:, 0:4096].bitcast(F32), PB_[:, 4096:8192].bitcast(F32)]
    guT_hi = [PE_[:, 0:2048], PE_[:, 4096:6144]]
    guT_lo = [PE_[:, 2048:4096], PE_[:, 6144:8192]]
    gsil = PE_[:, 8192:10240].bitcast(F32)
    umul = PE_[:, 10240:12288].bitcast(F32)
    hbuf = PE_[:, 12288:14336].bitcast(F32)
    fstage = PE_[:, 0:16384].bitcast(F32)
    dnT_hi = PD_[:, 0:2048]
    dnT_lo = PD_[:, 2048:4096]
    htm = PD_[:, 4096:6144].bitcast(F32)
    htm_hi = PD_[:, 6144:7168]
    htm_lo = PD_[:, 7168:8192]
    hout = [PD_[:, 8192:10240], PD_[:, 10240:12288]]
    hma = PD_[:, 12288:12352].bitcast(F32)
    hs2 = PD_[:, 12352:12416].bitcast(F32)
    hrq = PD_[:, 12416:12480].bitcast(F32)
    hq8 = PD_[:, 12480:12992].bitcast(FP8)
    deqf = [PK_[:, 0:4096].bitcast(F32), PK_[:, 4096:8192].bitcast(F32)]
    q8b = [PK_[:, 8192:9216].bitcast(FP8), PK_[:, 9216:10240].bitcast(FP8)]
    wma = [PK_[:, 10240 + 128 * i:10240 + 128 * (i + 1)].bitcast(F32)
           for i in range(4)]
    ws2 = [PK_[:, 10752 + 128 * i:10752 + 128 * (i + 1)].bitcast(F32)
           for i in range(4)]
    wrq = [PK_[:, 11264 + 128 * i:11264 + 128 * (i + 1)].bitcast(F32)
           for i in range(4)]
    xq_hi = PK_[:, 11776:13824]
    xq_lo = PK_[:, 13824:15872]
    graw = [PW_[:, 0:1024], PW_[:, 1024:2048]]
    uraw = [PW_[:, 2048:3072], PW_[:, 3072:4096]]
    fev = PB_[:].bitcast(F32)[:, 1024:1536]
    fev2 = PB_[:].bitcast(F32)[:, 1536:2048]
    deqf_ln2 = PE_[:, 8192:12288].bitcast(F32)
    q8_ln2 = PE_[:, 12288:13312].bitcast(FP8)

    pq = [psum("pq0", [128, 512]), psum("pq1", [128, 512])]
    pS = psum("pS", [128, 1024])
    pav = psum("pav", [128, 512])
    ptrb = psum("ptrb", [128, 1024], BF16)
    pst = psum("pst", [128, 512])
    pwo = psum("pwo", [128, 512])
    ptrf = pwo

    def dma(dst, src, wait=None):
        return p.run("sync", lambda e, d=dst, s=src: e.dma_start(out=d, in_=s),
                     inc=LD, incn=16, wait=wait)

    def mm3(psum_ap, pairs, start=True, stop=True, wait=None):
        ops = []
        for (lh, ll, rh, rl) in pairs:
            ops += [(lh, rh), (ll, rh), (lh, rl)]
        ret = None
        for j, (l, r) in enumerate(ops):
            st = start and (j == 0)
            sp = stop and (j == len(ops) - 1)
            ret = p.run("tensor", lambda e, l=l, r=r, st=st, sp=sp, o=psum_ap:
                        e.matmul(o, l, r, start=st, stop=sp),
                        inc=(TC if sp else None), wait=(wait if j == 0 else None))
        return ret

    wraw_free = [[("SC", 0)]]
    whl_free = [[("TC", 0)], [("TC", 0)]]

    def emit_wchunk(dram_slab, b):
        ldv = p.run("sync", lambda e, d=dram_slab: e.dma_start(out=wraw, in_=d),
                    inc=LD, incn=16, wait=wraw_free[0])
        sh = p.run("scalar", lambda e, o=wwhi[b]: e.activation(
            out=o, in_=wraw, func=ACT.Copy), inc=SC,
            wait=[("LD", ldv)] + whl_free[b])
        vl = p.run("vector", lambda e, hh=wwhi[b], oo=wwlo[b]:
                   e.tensor_sub(oo, wraw, hh), inc=VC, wait=[("SC", sh)])
        wraw_free[0] = [("VC", vl)]
        return vl

    pst_free = [("VC", 0)]

    def emit_norm_stats(src_tile, kind, first_wait=None):
        for fc in range(FC):
            sw = list(first_wait or []) if fc == 0 else []
            vprev = p.v(VC)
            sq = p.run("scalar",
                       lambda e, a=src_tile(fc): e.activation(out=sqb[:], in_=a,
                                                              func=ACT.Square),
                       inc=SC, wait=sw + [("VC", vprev)])
            if fc == 0:
                p.run("vector", lambda e, a=src_tile(0): e.tensor_copy(acc1[:], a),
                      inc=VC, wait=list(first_wait or []))
                p.run("vector", lambda e: e.tensor_copy(acc2[:], sqb[:]),
                      inc=VC, wait=[("SC", sq)])
            else:
                p.run("vector", lambda e, a=src_tile(fc):
                      e.tensor_add(acc1[:], acc1[:], a), inc=VC)
                p.run("vector", lambda e: e.tensor_add(acc2[:], acc2[:], sqb[:]),
                      inc=VC, wait=[("SC", sq)])
        vdone = p.v(VC)
        t1 = p.run("tensor", lambda e: e.matmul(pst[:], onesf[:], acc1[:],
                                                start=True, stop=True),
                   inc=TC, wait=[("VC", vdone), pst_free[0]])
        vm = p.run("vector", lambda e: e.tensor_scalar(m_t[:], pst[:], 1.0 / D,
                                                       None, ALU.mult),
                   inc=VC, wait=[("TC", t1)])
        t2 = p.run("tensor", lambda e: e.matmul(pst[:], onesf[:], acc2[:],
                                                start=True, stop=True),
                   inc=TC, wait=[("VC", vm)])
        ve = p.run("vector", lambda e: e.tensor_scalar(acc1[:], pst[:], 1.0 / D,
                                                       None, ALU.mult),
                   inc=VC, wait=[("TC", t2)])
        pst_free[0] = ("VC", ve)
        p.run("vector", lambda e: e.tensor_mul(t2_t[:], m_t[:], m_t[:]), inc=VC)
        vv = p.run("vector", lambda e: e.tensor_sub(acc1[:], acc1[:], t2_t[:]),
                   inc=VC)
        p.run("scalar", lambda e: e.activation(out=t2_t[:], in_=acc1[:],
                                               func=ACT.Ln, bias=epsln_t[:]),
              inc=SC, wait=[("VC", vv)])
        s2 = p.run("scalar", lambda e: e.activation(out=g_t[:], in_=t2_t[:],
                                                    func=ACT.Exp, scale=-0.5),
                   inc=SC)
        if kind == "ln_rms":
            p.run("vector", lambda e: e.tensor_mul(t2_t[:], g_t[:], g_t[:]),
                  inc=VC, wait=[("SC", s2)])
            vr2 = p.run("vector", lambda e: e.tensor_mul(t2_t[:], acc1[:], t2_t[:]),
                        inc=VC)
            p.run("scalar", lambda e: e.activation(out=t2_t[:], in_=t2_t[:],
                                                   func=ACT.Ln, bias=epsrms_t[:]),
                  inc=SC, wait=[("VC", vr2)])
            s4 = p.run("scalar", lambda e: e.activation(out=t2_t[:], in_=t2_t[:],
                                                        func=ACT.Exp, scale=-0.5),
                       inc=SC)
            gd = p.run("vector", lambda e: e.tensor_mul(g_t[:], g_t[:], t2_t[:]),
                       inc=VC, wait=[("SC", s4)])
            return ("VC", gd)
        return ("SC", s2)

    def emit_xn(src_tile, kind, first_wait=None):
        gd = emit_norm_stats(src_tile, kind, first_wait=first_wait)
        use_lnw = (kind == "ln_rms")
        for fc in range(FC):
            w = [gd] if fc == 0 else []
            sprev = p.v(SC)
            p.run("vector", lambda e, a=src_tile(fc): e.tensor_sub(ub[:], a, m_t[:]),
                  inc=VC, wait=w + [("SC", sprev)])
            vu = p.run("vector", lambda e: e.tensor_mul(ub[:], ub[:], g_t[:]),
                       inc=VC)
            if use_lnw:
                vu = p.run("vector", lambda e, f=fc: e.tensor_scalar(
                    ub[:], ub[:], lnw[:, f:f + 1], None, ALU.mult), inc=VC)
            sh = p.run("scalar", lambda e, f=fc: e.activation(
                out=xn_hi[:, f * STW:(f + 1) * STW], in_=ub[:], func=ACT.Copy),
                inc=SC, wait=[("VC", vu)])
            p.run("vector", lambda e, f=fc: e.tensor_sub(
                xn_lo[:, f * STW:(f + 1) * STW], ub[:],
                xn_hi[:, f * STW:(f + 1) * STW]), inc=VC, wait=[("SC", sh)])
        return p.v(VC)

    pq_free = [[("VC", 0)], [("VC", 0)]]
    ptr_free = [("VC", 0)]
    ptrf_free = [("VC", 0)]

    def emit_rope_evac(b, dst_hi, dst_lo, tmm):
        pqb = pq[b]
        sprev = p.v(SC)
        p.run("vector", lambda e: e.tensor_mul(tmp64a, pqb[0:64, :], cosb),
              inc=VC, wait=[("TC", tmm), ("SC", sprev)])
        p.run("vector", lambda e: e.tensor_mul(tmp64b, pqb[64:128, :], sinb),
              inc=VC)
        p.run("vector", lambda e: e.tensor_sub(ub[0:64, :], tmp64a, tmp64b),
              inc=VC)
        p.run("vector", lambda e: e.tensor_mul(tmp64a, pqb[64:128, :], cosb),
              inc=VC)
        vp = p.run("vector", lambda e: e.tensor_mul(tmp64b, pqb[0:64, :], sinb),
                   inc=VC)
        p.run("vector", lambda e: e.tensor_add(ub[64:128, :], tmp64a, tmp64b),
              inc=VC)
        pq_free[b] = [("VC", vp)]
        sh = p.run("scalar", lambda e, d=dst_hi: e.activation(out=d, in_=ub[:],
                                                              func=ACT.Copy),
                   inc=SC, wait=[("VC", p.v(VC))])
        vl = p.run("vector", lambda e, d=dst_lo, h2=dst_hi:
                   e.tensor_sub(d, ub[:], h2), inc=VC, wait=[("SC", sh)])
        return vl

    def emit_v_evac(b, kv, kvhalf, tmm, dodbg=False):
        pqb = pq[b]
        sprev = p.v(SC)
        sh = p.run("scalar", lambda e: e.activation(out=ub[:], in_=pqb[:],
                                                    func=ACT.Copy),
                   inc=SC, wait=[("TC", tmm), ("VC", p.v(VC)), ("SC", sprev)])
        if dodbg:
            dbg(ub[:])
        sh2 = p.run("scalar", lambda e: e.activation(out=vtmp_hi, in_=ub[:],
                                                     func=ACT.Copy), inc=SC)
        vlo = p.run("vector", lambda e: e.tensor_sub(vtmp_lo, ub[:], vtmp_hi),
                    inc=VC, wait=[("SC", sh2)])
        if dodbg:
            dbg(vtmp_hi)
        pq_free[b] = [("SC", sh)]
        tlast = None
        for j in range(4):
            tlast = p.run("tensor", lambda e, j=j: e.transpose(
                ptrb[:, j * 128:(j + 1) * 128], vtmp_hi[:, j * 128:(j + 1) * 128],
                identb[:]), inc=None,
                wait=([("VC", vlo), ptr_free[0]] if j == 0 else None))
        for j in range(4):
            tlast = p.run("tensor", lambda e, j=j: e.transpose(
                ptrb[:, 512 + j * 128:512 + (j + 1) * 128],
                vtmp_lo[:, j * 128:(j + 1) * 128], identb[:]),
                inc=(TC if j == 3 else None))
        vh3 = v_hi.rearrange("p (tt o) -> p tt o", tt=8)
        vl3 = v_lo.rearrange("p (tt o) -> p tt o", tt=8)
        pt3 = ptrb[:].rearrange("p (j o) -> p j o", j=8)
        p.run("vector", lambda e: e.tensor_copy(
            vh3[:, kvhalf * 4:kvhalf * 4 + 4, kv * 128:(kv + 1) * 128],
            pt3[:, 0:4, :]), inc=VC, wait=[("TC", tlast)])
        e2 = p.run("vector", lambda e: e.tensor_copy(
            vl3[:, kvhalf * 4:kvhalf * 4 + 4, kv * 128:(kv + 1) * 128],
            pt3[:, 4:8, :]), inc=VC)
        ptr_free[0] = ("VC", e2)
        return e2

    maskb_free = [0]

    def proj_chunk(dram_w, b, pairs_rhs, xn_ready, wv_=None):
        if wv_ is None:
            wv_ = emit_wchunk(dram_w, b)
        pairs = [(wwhi[b][:, fc * 128:(fc + 1) * 128],
                  wwlo[b][:, fc * 128:(fc + 1) * 128],
                  pairs_rhs[fc][0], pairs_rhs[fc][1]) for fc in range(FC)]
        tmm = mm3(pq[b][:], pairs, wait=[("VC", max(xn_ready, wv_))] + pq_free[b])
        whl_free[0] = whl_free[1] = [("TC", tmm)]
        return tmm

    def emit_A_kv(s):
        if s == 1:
            tprev = p.v(TC)
            for t in (k_hi, k_lo):
                k3 = t.rearrange("p (kv t) -> p kv t", kv=KH)
                p.run("vector", lambda e, k3=k3: e.tensor_copy(
                    k3[:, :, 0:STW], k3[:, :, STW:2 * STW]), inc=VC,
                    wait=[("TC", tprev)])
            for t in (v_hi, v_lo):
                p.run("vector", lambda e, t=t: e.tensor_copy(
                    t[:, 0:4 * STW], t[:, 4 * STW:8 * STW]), inc=VC)

        chunks = [("halo", 0), ("own", 1)] if s == 0 else [("own", 1)]
        for (cname, kvhalf) in chunks:
            is_own = (cname == "own")
            cs_off = STW + s * STW if is_own else 0
            if cname == "halo":
                pass
            ldc1 = dma(cosb, cos_d[:, cs_off:cs_off + STW],
                       wait=[("VC", p.v(VC))])
            ldc2 = dma(sinb, sin_d[:, cs_off:cs_off + STW])
            p.wait("vector", "LD", max(ldc1, ldc2))

            if is_own:
                def src(fc, off=s * STW):
                    return resid[:, fc * TPC + off: fc * TPC + off + STW]
                emit_xn(src, "ln_rms", first_wait=[("TC", p.v(TC))])
            else:
                emit_xn_halo()
            xn_ready = p.v(VC)
            if s == 0 and is_own:
                dbg(m_t[:])
                dbg(g_t[:])
                dbg(xn_hi[:, 0:512])
                dbg(xn_lo[:, 0:512])

            for kv in range(KH):
                b = kv % 2
                tmm = proj_chunk(wk_d[kv].rearrange("p f o -> p (f o)"), b,
                                 [(xn_hi[:, fc * STW:(fc + 1) * STW],
                                   xn_lo[:, fc * STW:(fc + 1) * STW])
                                  for fc in range(FC)], xn_ready)
                kb = kv * 1024 + kvhalf * STW
                emit_rope_evac(b, k_hi[:, kb:kb + STW], k_lo[:, kb:kb + STW], tmm)
                if s == 0 and is_own:
                    dbg(k_hi[:, kb:kb + STW])
                    dbg(xn_hi[:, 0:512])
            for kv in range(KH):
                b = kv % 2
                tmm = proj_chunk(wv_d[kv].rearrange("p f o -> p (f o)"), b,
                                 [(xn_hi[:, fc * STW:(fc + 1) * STW],
                                   xn_lo[:, fc * STW:(fc + 1) * STW])
                                  for fc in range(FC)], xn_ready)
                emit_v_evac(b, kv, kvhalf, tmm,
                            dodbg=(s == 0 and kv == 0 and is_own and DEBUG))
                if s == 0 and kv == 0 and is_own:
                    dbg(v_hi[:, 4 * 512:5 * 512])
        return p.v(VC)

    def emit_xn_halo():
        hx = PK_[:, 8192:16384].bitcast(F32)
        lds = [0, 0]
        lds[0] = dma(hx.rearrange("p (f t) -> p f t", f=8), xh_d[:, 0:8, :])

        def src_a(fc):
            return hx[:, fc * STW:(fc + 1) * STW]
        for half in range(2):
            if half == 1:
                lds[1] = dma(hx.rearrange("p (f t) -> p f t", f=8),
                             xh_d[:, 8:16, :], wait=[("VC", p.v(VC)),
                                                     ("SC", p.v(SC))])
            for fc8 in range(8):
                fw = [("LD", lds[half])] if fc8 == 0 else []
                fc = half * 8 + fc8
                vprev = p.v(VC)
                sq = p.run("scalar", lambda e, a=src_a(fc8): e.activation(
                    out=sqb[:], in_=a, func=ACT.Square), inc=SC,
                    wait=fw + [("VC", vprev)])
                if fc == 0:
                    p.run("vector", lambda e, a=src_a(0): e.tensor_copy(acc1[:], a),
                          inc=VC, wait=fw)
                    p.run("vector", lambda e: e.tensor_copy(acc2[:], sqb[:]),
                          inc=VC, wait=[("SC", sq)])
                else:
                    p.run("vector", lambda e, a=src_a(fc8):
                          e.tensor_add(acc1[:], acc1[:], a), inc=VC,
                          wait=fw)
                    p.run("vector", lambda e: e.tensor_add(acc2[:], acc2[:], sqb[:]),
                          inc=VC, wait=[("SC", sq)])
        vdone = p.v(VC)
        t1 = p.run("tensor", lambda e: e.matmul(pst[:], onesf[:], acc1[:],
                                                start=True, stop=True),
                   inc=TC, wait=[("VC", vdone), pst_free[0]])
        vm = p.run("vector", lambda e: e.tensor_scalar(m_t[:], pst[:], 1.0 / D,
                                                       None, ALU.mult),
                   inc=VC, wait=[("TC", t1)])
        t2 = p.run("tensor", lambda e: e.matmul(pst[:], onesf[:], acc2[:],
                                                start=True, stop=True),
                   inc=TC, wait=[("VC", vm)])
        ve = p.run("vector", lambda e: e.tensor_scalar(acc1[:], pst[:], 1.0 / D,
                                                       None, ALU.mult),
                   inc=VC, wait=[("TC", t2)])
        pst_free[0] = ("VC", ve)
        p.run("vector", lambda e: e.tensor_mul(t2_t[:], m_t[:], m_t[:]), inc=VC)
        vv = p.run("vector", lambda e: e.tensor_sub(acc1[:], acc1[:], t2_t[:]),
                   inc=VC)
        p.run("scalar", lambda e: e.activation(out=t2_t[:], in_=acc1[:],
                                               func=ACT.Ln, bias=epsln_t[:]),
              inc=SC, wait=[("VC", vv)])
        s2 = p.run("scalar", lambda e: e.activation(out=g_t[:], in_=t2_t[:],
                                                    func=ACT.Exp, scale=-0.5),
                   inc=SC)
        p.run("vector", lambda e: e.tensor_mul(t2_t[:], g_t[:], g_t[:]),
              inc=VC, wait=[("SC", s2)])
        vr2 = p.run("vector", lambda e: e.tensor_mul(t2_t[:], acc1[:], t2_t[:]),
                    inc=VC)
        p.run("scalar", lambda e: e.activation(out=t2_t[:], in_=t2_t[:],
                                               func=ACT.Ln, bias=epsrms_t[:]),
              inc=SC, wait=[("VC", vr2)])
        s4 = p.run("scalar", lambda e: e.activation(out=t2_t[:], in_=t2_t[:],
                                                    func=ACT.Exp, scale=-0.5),
                   inc=SC)
        gd = p.run("vector", lambda e: e.tensor_mul(g_t[:], g_t[:], t2_t[:]),
                   inc=VC, wait=[("SC", s4)])
        for half in range(2):
            ldh = dma(hx.rearrange("p (f t) -> p f t", f=8),
                      xh_d[:, half * 8:(half + 1) * 8, :],
                      wait=[("VC", p.v(VC)), ("SC", p.v(SC))])
            for fc8 in range(8):
                fc = half * 8 + fc8
                w = [("VC", gd), ("LD", ldh)] if fc8 == 0 else []
                sprev = p.v(SC)
                p.run("vector", lambda e, a=src_a(fc8): e.tensor_sub(
                    ub[:], a, m_t[:]), inc=VC, wait=w + [("SC", sprev)])
                p.run("vector", lambda e: e.tensor_mul(ub[:], ub[:], g_t[:]),
                      inc=VC)
                vu = p.run("vector", lambda e, f=fc: e.tensor_scalar(
                    ub[:], ub[:], lnw[:, f:f + 1], None, ALU.mult), inc=VC)
                sh = p.run("scalar", lambda e, f=fc: e.activation(
                    out=xn_hi[:, f * STW:(f + 1) * STW], in_=ub[:],
                    func=ACT.Copy), inc=SC, wait=[("VC", vu)])
                p.run("vector", lambda e, f=fc: e.tensor_sub(
                    xn_lo[:, f * STW:(f + 1) * STW], ub[:],
                    xn_hi[:, f * STW:(f + 1) * STW]), inc=VC,
                    wait=[("SC", sh)])

    def emit_qbatch(s, hb, xn_ready):
        for hh in range(8):
            h = hb * 8 + hh
            b = hh % 2
            tmm = proj_chunk(wq_d[h].rearrange("p f o -> p (f o)"), b,
                             [(xn_hi[:, fc * STW:(fc + 1) * STW],
                               xn_lo[:, fc * STW:(fc + 1) * STW])
                              for fc in range(FC)], xn_ready)
            emit_rope_evac(b, q_hi[:, hh * STW:(hh + 1) * STW],
                           q_lo[:, hh * STW:(hh + 1) * STW], tmm)
            if s == 0 and hb == 0 and hh == 0:
                dbg(ub[:])
                dbg(q_hi[:, 0:512])

    pav_free = [[("VC", 0)]]
    pS_free = [("VC", 0)]

    def emit_B(s, hb):
        nonlocal_vmask = [0]
        if hb == 0:
            ldm = dma(maskbuf.rearrange("p (a w) -> p a w", a=4), mask_d[s],
                      wait=[("VC", p.v(VC)), ("TC", p.v(TC)), ("SC", p.v(SC))])
            p.wait("vector", "LD", ldm)
        for hh in range(8):
            h = hb * 8 + hh
            kvh = h // NPER
            for a in range(4):
                qh = q_hi[:, hh * STW + a * 128: hh * STW + (a + 1) * 128]
                ql = q_lo[:, hh * STW + a * 128: hh * STW + (a + 1) * 128]
                kb = kvh * 1024 + a * 128
                mm3(pS[:, 0:512],
                    [(qh, ql, k_hi[:, kb:kb + 512], k_lo[:, kb:kb + 512])],
                    wait=[pS_free[0]])
                t_s = mm3(pS[:, 512:640],
                          [(qh, ql, k_hi[:, kb + 512:kb + 640],
                            k_lo[:, kb + 512:kb + 640])])
                vm = p.run("vector", lambda e, a=a: e.scalar_tensor_tensor(
                    ssb, maskbuf[:, a * 640:(a + 1) * 640], NEG_BIG,
                    pS[:, 0:640], ALU.mult, ALU.add),
                    inc=VC, wait=[("TC", t_s)])
                pS_free[0] = ("VC", vm)
                nonlocal_vmask[0] = vm
                p.run("vector", lambda e: e.reduce_max(nmax[:], ssb, axis=AX.X,
                                                       negate=True), inc=VC)
                p.run("vector", lambda e: e.tensor_scalar(
                    nmaxs[:], nmax[:], SCALE_QK, None, ALU.mult), inc=VC)
                se = p.run("scalar", lambda e: e.activation(
                    out=esb, in_=ssb, func=ACT.Exp, bias=nmaxs[:],
                    scale=SCALE_QK, accum_out=sume[:]), inc=SC,
                    wait=[("VC", p.v(VC))])
                p.run("vector", lambda e: e.reciprocal(rsum[:], sume[:]),
                      inc=VC, wait=[("SC", se)])
                p.run("vector", lambda e: e.tensor_scalar(
                    p_hi, esb, rsum[:], None, ALU.mult), inc=VC)
                vl = p.run("vector", lambda e: e.scalar_tensor_tensor(
                    p_lo, esb, rsum[:], p_hi, ALU.mult, ALU.subtract),
                    inc=VC)
                if s == 0 and hb == 0 and hh == 0 and a == 0:
                    dbg(ssb[:, 0:512])
                    dbg(esb[:, 0:512])
                    dbg(p_hi[:])
                for (srcb, dstb) in ((p_hi, pt_hi), (p_lo, pt_lo)):
                    tl = None
                    tws = [("VC", vl), ptr_free[0]]
                    for j in range(5):
                        tl = p.run("tensor", lambda e, j=j, srcb=srcb: e.transpose(
                            ptrb[:, j * 128:(j + 1) * 128],
                            srcb[:, j * 128:(j + 1) * 128], identb[:]),
                            inc=(TC if j == 4 else None),
                            wait=(tws if j == 0 else None))
                    ev = None
                    for j in range(5):
                        c = a + j
                        idx = a - max(0, c - 4)
                        col = (AOFF[c] + idx) * 128
                        ev = p.run("vector", lambda e, j=j, col=col, dstb=dstb:
                                   e.tensor_copy(dstb[:, col:col + 128],
                                                 ptrb[:, j * 128:(j + 1) * 128]),
                                   inc=VC, wait=([("TC", tl)] if j == 0 else None))
                    ptr_free[0] = ("VC", ev)
            vtrans_done = p.v(VC)
            tav = None
            first = True
            for c in range(8):
                n = ACNT[c]
                a0 = max(0, c - 4)
                colr = slice(AOFF[c] * 128, (AOFF[c] + n) * 128)
                vh_t = v_hi[:, c * 512 + kvh * 128: c * 512 + (kvh + 1) * 128]
                vl_t = v_lo[:, c * 512 + kvh * 128: c * 512 + (kvh + 1) * 128]
                out_ap = pav[:, a0 * 128:(a0 + n) * 128]
                ops = [(vh_t, pt_hi[:, colr]), (vl_t, pt_hi[:, colr]),
                       (vh_t, pt_lo[:, colr])]
                for j, (l, r) in enumerate(ops):
                    st = first
                    first = False
                    sp = (c == 7 and j == 2)
                    tav = p.run("tensor", lambda e, l=l, r=r, st=st, sp=sp,
                                o=out_ap: e.matmul(o, l, r, start=st, stop=sp),
                                inc=(TC if sp else None),
                                wait=([("VC", vtrans_done)] + pav_free[0]
                                      if (c == 0 and j == 0) else None))
            sh = p.run("scalar", lambda e, h=h: e.activation(
                out=ao_hi(h), in_=pav[:], func=ACT.Copy),
                inc=SC, wait=[("TC", tav)])
            vl2 = p.run("vector", lambda e, h=h: e.tensor_sub(
                ao_lo(h), pav[:], ao_hi(h)), inc=VC, wait=[("SC", sh)])
            pav_free[0] = [("VC", vl2)]
            if s == 0 and h == 0:
                dbg(ao_hi(0))
        maskb_free[0] = nonlocal_vmask[0]

    def emit_C(s):
        pwo_free = [("SC", 0)]
        for th in range(2):
            for dc in range(DC):
                b = dc % 2
                wv_ = emit_wchunk(wo_d[dc].rearrange("p f o -> p (f o)"), b)
                pairs = [(wwhi[b][:, h * 128:(h + 1) * 128],
                          wwlo[b][:, h * 128:(h + 1) * 128],
                          ao_hi(h)[:, th * 256:(th + 1) * 256],
                          ao_lo(h)[:, th * 256:(th + 1) * 256]) for h in range(H)]
                tmm = mm3(pwo[:, 0:256], pairs,
                          wait=[("VC", wv_)] + pwo_free)
                whl_free[0] = whl_free[1] = [("TC", tmm)]
                sh = p.run("scalar", lambda e, dc=dc: e.activation(
                    out=attn_out[:, dc * 256:(dc + 1) * 256], in_=pwo[:, 0:256],
                    func=ACT.Copy), inc=SC, wait=[("TC", tmm)])
                pwo_free = [("SC", sh)]
            sdone = p.v(SC)

            def ao_tile(fc):
                return attn_out[:, fc * 256:(fc + 1) * 256]
            gd = emit_norm_stats256(ao_tile, first_wait=[("SC", sdone)])
            own = s * STW + th * 256
            for fc in range(FC):
                w = [gd] if fc == 0 else []
                p.run("vector", lambda e, a=ao_tile(fc): e.tensor_sub(
                    ub[:, 0:256], a, m_t[:, 0:256]), inc=VC, wait=w)
                p.run("vector", lambda e: e.tensor_mul(
                    ub[:, 0:256], ub[:, 0:256], g_t[:, 0:256]), inc=VC)
                p.run("vector", lambda e, fc=fc, own=own: e.tensor_add(
                    resid[:, fc * TPC + own: fc * TPC + own + 256],
                    resid[:, fc * TPC + own: fc * TPC + own + 256],
                    ub[:, 0:256]), inc=VC)
        ptrf_free[0] = ("SC", p.v(SC))
        if s == 0:
            dbg(attn_out[:, 0:256])
            dbg(resid[:, 0:512])

    def emit_norm_stats256(src_tile, first_wait=None):
        for fc in range(FC):
            sw = list(first_wait or []) if fc == 0 else []
            vprev = p.v(VC)
            sq = p.run("scalar",
                       lambda e, a=src_tile(fc): e.activation(
                           out=sqb[:, 0:256], in_=a, func=ACT.Square),
                       inc=SC, wait=sw + [("VC", vprev)])
            if fc == 0:
                p.run("vector", lambda e, a=src_tile(0): e.tensor_copy(
                    acc1[:, 0:256], a), inc=VC, wait=list(first_wait or []))
                p.run("vector", lambda e: e.tensor_copy(acc2[:, 0:256],
                                                        sqb[:, 0:256]),
                      inc=VC, wait=[("SC", sq)])
            else:
                p.run("vector", lambda e, a=src_tile(fc):
                      e.tensor_add(acc1[:, 0:256], acc1[:, 0:256], a), inc=VC)
                p.run("vector", lambda e: e.tensor_add(
                    acc2[:, 0:256], acc2[:, 0:256], sqb[:, 0:256]),
                    inc=VC, wait=[("SC", sq)])
        vdone = p.v(VC)
        t1 = p.run("tensor", lambda e: e.matmul(pst[:, 0:256], onesf[:],
                                                acc1[:, 0:256],
                                                start=True, stop=True),
                   inc=TC, wait=[("VC", vdone), pst_free[0]])
        vm = p.run("vector", lambda e: e.tensor_scalar(
            m_t[:, 0:256], pst[:, 0:256], 1.0 / D, None, ALU.mult),
            inc=VC, wait=[("TC", t1)])
        t2 = p.run("tensor", lambda e: e.matmul(pst[:, 0:256], onesf[:],
                                                acc2[:, 0:256],
                                                start=True, stop=True),
                   inc=TC, wait=[("VC", vm)])
        ve = p.run("vector", lambda e: e.tensor_scalar(
            acc1[:, 0:256], pst[:, 0:256], 1.0 / D, None, ALU.mult),
            inc=VC, wait=[("TC", t2)])
        pst_free[0] = ("VC", ve)
        p.run("vector", lambda e: e.tensor_mul(t2_t[:, 0:256], m_t[:, 0:256],
                                               m_t[:, 0:256]), inc=VC)
        vv = p.run("vector", lambda e: e.tensor_sub(acc1[:, 0:256],
                                                    acc1[:, 0:256],
                                                    t2_t[:, 0:256]), inc=VC)
        p.run("scalar", lambda e: e.activation(out=t2_t[:, 0:256],
                                               in_=acc1[:, 0:256],
                                               func=ACT.Ln, bias=epsln_t[:]),
              inc=SC, wait=[("VC", vv)])
        s2 = p.run("scalar", lambda e: e.activation(out=g_t[:, 0:256],
                                                    in_=t2_t[:, 0:256],
                                                    func=ACT.Exp, scale=-0.5),
                   inc=SC)
        return ("SC", s2)

    def emit_mx_dequant(raw_ap, q8_ap, deq_ap, ma_ap, s2_ap, rq_ap, nblk, wait):
        r3 = raw_ap.rearrange("p (g j) -> p g j", j=32)
        p.run("vector", lambda e: e.reduce_max(
            ma_ap[:, 0:nblk].unsqueeze(2), r3, axis=AX.X,
            apply_absolute_value=True), inc=VC, wait=wait)
        p.run("vector", lambda e: e.tensor_scalar(
            s2_ap[:, 0:nblk], ma_ap[:, 0:nblk], INV448x2, 2e-30,
            ALU.mult, ALU.max), inc=VC)
        p.run("vector", lambda e: e.reciprocal(rq_ap[:, 0:nblk], s2_ap[:, 0:nblk]),
              inc=VC)
        q3 = q8_ap.rearrange("p (g j) -> p g j", j=32)
        sbc = s2_ap[:, 0:nblk].unsqueeze(2).broadcast_to([128, nblk, 32])
        rbc = rq_ap[:, 0:nblk].unsqueeze(2).broadcast_to([128, nblk, 32])
        p.run("vector", lambda e: e.tensor_tensor(q3, r3, rbc, ALU.mult), inc=VC)
        d3 = deq_ap.rearrange("p (g j) -> p g j", j=32)
        vd = p.run("vector", lambda e: e.tensor_tensor(d3, q3, sbc, ALU.mult),
                   inc=VC)
        return vd

    def emit_T_split_f32(deq_ap, hiT_ap, loT_ap, nt, vwait):
        lastv = None
        for b0 in range(0, nt, 4):
            nb = min(4, nt - b0)
            tl = None
            for j in range(nb):
                w = None
                if j == 0:
                    w = [ptrf_free[0]]
                    if b0 == 0:
                        w.append(("VC", vwait))
                tl = p.run("tensor", lambda e, j=j, b0=b0: e.transpose(
                    ptrf[:, j * 128:(j + 1) * 128],
                    deq_ap[:, (b0 + j) * 128:(b0 + j + 1) * 128], identf[:]),
                    inc=(TC if j == nb - 1 else None), wait=w)
            sh = p.run("scalar", lambda e, b0=b0, nb=nb: e.activation(
                out=hiT_ap[:, b0 * 128:(b0 + nb) * 128], in_=ptrf[:, 0:nb * 128],
                func=ACT.Copy), inc=SC, wait=[("TC", tl)])
            lastv = p.run("vector", lambda e, b0=b0, nb=nb: e.tensor_sub(
                loT_ap[:, b0 * 128:(b0 + nb) * 128], ptrf[:, 0:nb * 128],
                hiT_ap[:, b0 * 128:(b0 + nb) * 128]), inc=VC, wait=[("SC", sh)])
            ptrf_free[0] = ("VC", lastv)
        return lastv

    def emit_ffn(lyr):
        stv = p.run("sync", lambda e: e.dma_start(out=rspill_d, in_=resid),
                    inc=ST_, incn=16, wait=[("VC", p.v(VC))])
        for ch in range(2):
            off = ch * STW

            def rtile(fc, off=off):
                return resid[:, fc * TPC + off: fc * TPC + off + STW]
            gd = emit_norm_stats(rtile, "ln",
                                 first_wait=([("ST", stv), ("TC", p.v(TC))]
                                             if ch == 0 else []))
            if ch == 0 and DEBUG:
                dbg(resid[:, 0:512])
                dbg(resid[:, 15 * TPC:15 * TPC + 512])
                dbg(m_t[:])
                dbg(g_t[:])
            for th in range(2):
                xst = PK_[:, 0:8192].bitcast(F32)
                for fc in range(FC):
                    w = [gd] if (fc == 0 and th == 0) else []
                    sprev = p.v(SC)
                    p.run("vector", lambda e, a=rtile(fc)[:, th * 256:(th + 1) * 256],
                          mm=m_t[:, th * 256:(th + 1) * 256]: e.tensor_sub(
                        ub[:, 0:256], a, mm), inc=VC,
                        wait=w + [("SC", sprev)])
                    vn = p.run("vector", lambda e, th=th: e.tensor_mul(
                        ub[:, 0:256], ub[:, 0:256],
                        g_t[:, th * 256:(th + 1) * 256]), inc=VC)
                    tl = None
                    for j in range(2):
                        tl = p.run("tensor", lambda e, j=j: e.transpose(
                            ptrf[:, j * 128:(j + 1) * 128],
                            ub[:, j * 128:(j + 1) * 128], identf[:]),
                            inc=(TC if j == 1 else None),
                            wait=([("VC", vn), ptrf_free[0]] if j == 0 else None))
                    ve = p.run("vector", lambda e, fc=fc: e.tensor_copy(
                        xst.rearrange("p (tt f) -> p tt f", tt=2)[
                            :, :, fc * 128:(fc + 1) * 128],
                        ptrf[:, 0:256].rearrange("p (j f) -> p j f", j=2)),
                        inc=VC, wait=[("TC", tl)])
                    ptrf_free[0] = ("VC", ve)
                if ch == 0 and DEBUG:
                    dbg(ub[:, 0:256])
                    dbg(xst[:, 0:512])
                for tt in range(2):
                    raw = xst[:, tt * 2048:(tt + 1) * 2048]
                    vd = emit_mx_dequant(raw, q8_ln2, deqf_ln2,
                        wma[3], ws2[3], wrq[3], 64, wait=[("SC", p.v(SC))])
                    sh = p.run("scalar", lambda e: e.activation(
                        out=xq_hi, in_=deqf_ln2, func=ACT.Copy), inc=SC,
                        wait=[("VC", vd)])
                    vlo = p.run("vector", lambda e: e.tensor_sub(
                        xq_lo, deqf_ln2, xq_hi), inc=VC, wait=[("SC", sh)])
                    if ch == 0 and tt == 0 and DEBUG:
                        dbg(deqf_ln2[:, 0:512])
                        dbg(xq_hi[:, 0:512])
                    gtt = th * 2 + tt
                    for (srcb, dstf) in ((xq_hi, xn2hi), (xq_lo, xn2lo)):
                        for b0 in range(0, 16, 8):
                            tl = None
                            for j in range(8):
                                w = None
                                if j == 0:
                                    w = [ptr_free[0], ("VC", vlo)]
                                tl = p.run("tensor",
                                           lambda e, j=j, b0=b0, srcb=srcb:
                                           e.transpose(
                                               ptrb[:, j * 128:(j + 1) * 128],
                                               srcb[:, (b0 + j) * 128:
                                                    (b0 + j + 1) * 128],
                                               identb[:]),
                                           inc=(TC if j == 7 else None), wait=w)
                            ev = None
                            for j in range(8):
                                fc = b0 + j
                                dstap = dstf(fc, ch)[:, gtt // 2,
                                                     (gtt % 2) * 128:
                                                     (gtt % 2) * 128 + 128]
                                ev = p.run("vector",
                                           lambda e, j=j, dstap=dstap:
                                           e.tensor_copy(
                                               dstap,
                                               ptrb[:, j * 128:(j + 1) * 128]),
                                           inc=VC,
                                           wait=([("TC", tl)] if j == 0 else None))
                            ptr_free[0] = ("VC", ev)

        dbg(PAb[:, 0:512])
        emit_ffn_pass(lyr, 0)
        for pi in range(1, len(DPASS)):
            emit_ffn_pass(lyr, pi)
        emit_ffn_epilogue(lyr)


    def emit_ffn_pass(lyr, pi):
        d0, d1 = DPASS[pi]
        nd = d1 - d0
        if pi == 0:
            accb = [pS[:, 0:512], pS[:, 512:1024], pav[:], pst[:]]
        else:
            accb = [pS[:, 0:512], pS[:, 512:1024], pav[:], pst[:],
                    pq[0][:], pq[1][:]]
        bank = {}
        idx = 0
        for dc in range(d0, d1):
            for tch in range(2):
                bank[(dc, tch)] = accb[idx]
                idx += 1
        graw_free = [[("VC", 0)], [("VC", 0)]]
        uraw_free = [[("VC", 0)], [("VC", 0)]]
        draw_free = [[("VC", 0)], [("VC", 0)]]
        guT_free = [("TC", 0)]
        dnT_free = [("TC", 0)]
        pg_free = [("SC", 0)]
        pu_free = [("SC", 0)]
        gsil_guard = [("VC", 0)]
        htm_guard = [("TC", 0)]
        hq_guard = [("TC", 0)]
        hout_free = [[("ST", 0)], [("ST", 0)]]
        dn_tc = [("TC", 0)]
        tprev_all = p.v(TC)
        vprev_all = p.v(VC)
        sprev_all = p.v(SC)

        for ic in range(ICN):
            bb = ic % 2
            ldd = dma(draw[bb][:, 0:nd * 128],
                      dw_d[ic].rearrange("p a b -> p (a b)")[:, d0 * 128:d1 * 128],
                      wait=draw_free[bb] + ([("TC", tprev_all),
                                             ("VC", vprev_all)]
                                            if ic == 0 else []))
            vdd = emit_mx_dequant(draw[bb][:, 0:nd * 128], q8b[0][:, 0:nd * 128],
                                  deqf[0][:, 0:nd * 128], wma[0], ws2[0], wrq[0],
                                  nd * 4, wait=[("LD", ldd), dn_tc[0]])
            draw_free[bb] = [("VC", vdd)]
            emit_T_split_f32(deqf[0][:, 0:nd * 128], dnT_hi[:, 0:nd * 128],
                             dnT_lo[:, 0:nd * 128], nd, vdd)
            dn_tc[0] = ("TC", p.v(TC))
            wTd_done = p.v(VC)

            if pi == 0:
                ldg = [dma(graw[hh], gw_d[ic, :, hh * 1024:(hh + 1) * 1024],
                           wait=graw_free[hh] + ([("TC", tprev_all)]
                                                 if ic == 0 else []))
                       for hh in range(2)]
                ldu = [dma(uraw[hh], uw_d[ic, :, hh * 1024:(hh + 1) * 1024],
                           wait=uraw_free[hh]) for hh in range(2)]
                for (which, ld2, rawb, hiT, loT, frees) in (
                        (0, ldg, graw, guT_hi[0], guT_lo[0], graw_free),
                        (1, ldu, uraw, guT_hi[1], guT_lo[1], uraw_free)):
                    for hh in range(2):
                        vd = emit_mx_dequant(
                            rawb[hh], q8b[1][:, 0:1024], deqf[1][:, 0:1024],
                            wma[1 + which], ws2[1 + which], wrq[1 + which], 32,
                            wait=[("LD", ld2[hh]), guT_free[0],
                                  ("SC", p.v(SC))])
                        frees[hh] = [("VC", vd)]
                        emit_T_split_f32(deqf[1][:, 0:1024],
                                         hiT[:, hh * 1024:(hh + 1) * 1024],
                                         loT[:, hh * 1024:(hh + 1) * 1024], 8, vd)
                wT_done = p.v(VC)
                for tch in range(2):
                    pairs_g = [(guT_hi[0][:, fc * 128:(fc + 1) * 128],
                                guT_lo[0][:, fc * 128:(fc + 1) * 128],
                                xn2hi(fc, tch), xn2lo(fc, tch))
                               for fc in range(FC)]
                    tg = mm3(pq[0][:], pairs_g,
                             wait=[("VC", wT_done), pg_free[0]])
                    sg = p.run("scalar", lambda e, tch=tch: e.activation(
                        out=gsil[:, tch * 512:(tch + 1) * 512], in_=pq[0][:],
                        func=ACT.Silu), inc=SC,
                        wait=[("TC", tg), gsil_guard[0]])
                    pg_free[0] = ("SC", sg)
                    pairs_u = [(guT_hi[1][:, fc * 128:(fc + 1) * 128],
                                guT_lo[1][:, fc * 128:(fc + 1) * 128],
                                xn2hi(fc, tch), xn2lo(fc, tch))
                               for fc in range(FC)]
                    tu = mm3(pq[1][:], pairs_u, wait=[pu_free[0]])
                    su = p.run("scalar", lambda e, tch=tch: e.activation(
                        out=umul[:, tch * 512:(tch + 1) * 512], in_=pq[1][:],
                        func=ACT.Copy), inc=SC, wait=[("TC", tu)])
                    pu_free[0] = ("SC", su)
                guT_free[0] = ("TC", p.v(TC))
                vh_ = p.run("vector", lambda e: e.tensor_mul(hbuf[:], gsil[:],
                                                             umul[:]),
                            inc=VC, wait=[("SC", p.v(SC)), hq_guard[0]])
                gsil_guard[0] = ("VC", vh_)
                for b0 in (0, 4):
                    tl = None
                    for j in range(4):
                        w = None
                        if j == 0:
                            w = [ptrf_free[0]]
                            if b0 == 0:
                                w.append(("VC", vh_))
                        tl = p.run("tensor", lambda e, j=j, b0=b0: e.transpose(
                            ptrf[:, j * 128:(j + 1) * 128],
                            hbuf[:, (b0 + j) * 128:(b0 + j + 1) * 128],
                            identf[:]), inc=(TC if j == 3 else None), wait=w)
                    ve = p.run("vector", lambda e, b0=b0: e.tensor_copy(
                        htm[:, b0 * 128:(b0 + 4) * 128], ptrf[:, 0:512]),
                        inc=VC, wait=[("TC", tl)])
                    ptrf_free[0] = ("VC", ve)
                vdh = emit_mx_dequant(htm, hq8[:, 0:1024], htm, hma, hs2, hrq,
                                      32, wait=None)
                sh = p.run("scalar", lambda e: e.activation(
                    out=htm_hi, in_=htm, func=ACT.Copy), inc=SC,
                    wait=[("VC", vdh), htm_guard[0]])
                vlo = p.run("vector", lambda e: e.tensor_sub(
                    htm_lo, htm, htm_hi), inc=VC, wait=[("SC", sh)])
                hob = hout[bb]
                for (srcb, dof) in ((htm_hi, 0), (htm_lo, 1024)):
                    tl = None
                    for j in range(8):
                        w = None
                        if j == 0:
                            w = [ptr_free[0], ("VC", vlo)] + hout_free[bb]
                        tl = p.run("tensor", lambda e, j=j, srcb=srcb:
                                   e.transpose(ptrb[:, j * 128:(j + 1) * 128],
                                               srcb[:, j * 128:(j + 1) * 128],
                                               identb[:]),
                                   inc=(TC if j == 7 else None), wait=w)
                    ev = p.run("vector", lambda e, hob=hob, dof=dof:
                               e.tensor_copy(hob[:, dof:dof + 1024],
                                             ptrb[:, 0:1024]), inc=VC,
                               wait=[("TC", tl)])
                    ptr_free[0] = ("VC", ev)
                htm_guard[0] = ("TC", p.v(TC))
                h_done = p.v(VC)
                sth = p.run("sync", lambda e, ic=ic, hob=hob: e.dma_start(
                    out=hsp_d[ic], in_=hob), inc=ST_, incn=16,
                    wait=[("VC", h_done)])
                hout_free[bb] = [("ST", sth)]
                h_hi_ap = hob[:, 0:1024]
                h_lo_ap = hob[:, 1024:2048]
                h_gate = ("VC", h_done)
                if ic == 0:
                    dbg(gsil[:, 0:512])
                    dbg(hob[:, 0:512])
            else:
                ldh = dma(hout[bb], hsp_d[ic], wait=hout_free[bb])
                hout_free[bb] = [("TC", 0)]
                h_hi_ap = hout[bb][:, 0:1024]
                h_lo_ap = hout[bb][:, 1024:2048]
                h_gate = ("LD", ldh)

            tdn = None
            for dc in range(d0, d1):
                for tch in range(2):
                    pairs = [(dnT_hi[:, (dc - d0) * 128:(dc - d0 + 1) * 128],
                              dnT_lo[:, (dc - d0) * 128:(dc - d0 + 1) * 128],
                              h_hi_ap[:, tch * 512:(tch + 1) * 512],
                              h_lo_ap[:, tch * 512:(tch + 1) * 512])]
                    w = [("VC", wTd_done), h_gate] if (dc == d0 and tch == 0) \
                        else None
                    tdn = mm3(bank[(dc, tch)], pairs,
                              start=(ic == 0), stop=(ic == ICN - 1), wait=w)
            if pi == 0:
                hout_free[bb] = hout_free[bb] + [("TC", tdn)]
            else:
                hout_free[bb] = [("TC", tdn)]
        fev_free = [("ST", 0), ("ST", 0)]
        tall = p.v(TC)
        for i2, ((dc, tch), bk) in enumerate(sorted(bank.items())):
            fb = [fev, fev2][i2 % 2]
            sv = p.run("scalar", lambda e, bk=bk, fb=fb: e.activation(
                out=fb, in_=bk, func=ACT.Copy), inc=SC,
                wait=[("TC", tall), fev_free[i2 % 2]])
            st2 = p.run("sync", lambda e, dc=dc, tch=tch, fb=fb: e.dma_start(
                out=fout_d[:, dc * 1024 + tch * 512: dc * 1024 + (tch + 1) * 512],
                in_=fb), inc=ST_, incn=16, wait=[("SC", sv)])
            fev_free[i2 % 2] = ("ST", st2)
        p.wait("tensor", "ST", p.v(ST_))

    def emit_ffn_epilogue(lyr):
        stall = p.v(ST_)
        for ch in range(2):
            lds = []
            for dc in range(DC):
                lds.append(dma(fstage[:, dc * STW:(dc + 1) * STW],
                               fout_d[:, dc * 1024 + ch * STW:
                                      dc * 1024 + ch * STW + STW],
                               wait=([("ST", stall), ("TC", p.v(TC)),
                                      ("VC", p.v(VC)), ("SC", p.v(SC))]
                                     if (ch == 0 and dc == 0) else None)))
            for fc in range(FC):
                lds.append(dma(resid[:, fc * TPC + ch * STW:
                                     fc * TPC + ch * STW + STW],
                               rspill_d[:, fc * TPC + ch * STW:
                                        fc * TPC + ch * STW + STW]))
            ldall = max(lds)
            p.wait("vector", "LD", ldall)
            p.wait("scalar", "LD", ldall)

            def ftile(fc):
                return fstage[:, fc * STW:(fc + 1) * STW]
            if ch == 0:
                dbg(fstage[:, 0:512], extra=[("LD", ldall)])
            gd = emit_norm_stats(ftile, "ln")
            for fc in range(FC):
                w = [gd] if fc == 0 else []
                p.run("vector", lambda e, a=ftile(fc): e.tensor_sub(
                    ub[:], a, m_t[:]), inc=VC, wait=w)
                p.run("vector", lambda e: e.tensor_mul(ub[:], ub[:], g_t[:]),
                      inc=VC)
                p.run("vector", lambda e, fc=fc, ch=ch: e.tensor_add(
                    resid[:, fc * TPC + ch * STW: fc * TPC + ch * STW + STW],
                    resid[:, fc * TPC + ch * STW: fc * TPC + ch * STW + STW],
                    ub[:]), inc=VC)

    p.run("vector", lambda e: e.memset(epsln_t[:], EPS_LN), inc=VC)
    p.run("vector", lambda e: e.memset(epsrms_t[:], EPS_RMS), inc=VC)
    dma(identb[:], identb_d)
    dma(identf[:], identf_d)
    dma(onesf[:], onesf_d)
    ld_ident = dma(lnw[:], lnw_d)
    ldr = dma(resid.rearrange("p (f t) -> p f t", f=FC), xo_d)
    dbg(identb[:], extra=[("LD", ld_ident)])
    dbg(onesf[:], extra=[("LD", ld_ident)])
    dbg(resid[:, 0:512], extra=[("LD", ldr)])
    p.wait("vector", "LD", ldr)
    p.wait("scalar", "LD", ldr)
    p.wait("tensor", "LD", ldr)

    for lyr in range(depth):
        for s in (0, 1):
            emit_A_kv(s)
            xn_ready = p.v(VC)
            for hb in range(2):
                emit_qbatch(s, hb, xn_ready)
                emit_B(s, hb)
            emit_C(s)
        emit_ffn(lyr)

    stv = p.run("sync", lambda e: e.dma_start(
        out=xout_d, in_=resid.rearrange("p (f t) -> p f t", f=FC)),
        inc=ST_, incn=16,
        wait=[("VC", p.v(VC)), ("SC", p.v(SC)), ("TC", p.v(TC))])
    p.wait("sync", "ST", stv)

    block = es.enter_context(nc.Block())

    @block.sync
    def _(e):
        for f in p.streams["sync"]:
            f(e)

    @block.vector
    def _(e):
        for f in p.streams["vector"]:
            f(e)

    @block.scalar
    def _(e):
        for f in p.streams["scalar"]:
            f(e)

    @block.tensor
    def _(e):
        for f in p.streams["tensor"]:
            f(e)

    @block.gpsimd
    def _(e):
        for f in p.streams["gpsimd"]:
            f(e)

    es.close()
    return nc


def kernel(L0_attn_x, L0_attn_ln_w, L0_attn_wq, L0_attn_wk, L0_attn_wv,
           L0_attn_wo, L0_ffn_gate_w, L0_ffn_up_w, L0_ffn_down_w):
    per_core = _prep_inputs(np.asarray(L0_attn_x), np.asarray(L0_attn_ln_w),
                            np.asarray(L0_attn_wq), np.asarray(L0_attn_wk),
                            np.asarray(L0_attn_wv), np.asarray(L0_attn_wo),
                            np.asarray(L0_ffn_gate_w), np.asarray(L0_ffn_up_w),
                            np.asarray(L0_ffn_down_w))
    nc = build_program(depth=1)
    res = run_bass_kernel_spmd(nc, per_core, core_ids=list(range(NCORES)))
    cols = []
    for c in range(NCORES):
        xo = np.asarray(res.results[c]["xout"])
        cols.append(xo.transpose(1, 0, 2).reshape(D, TPC))
    full = np.concatenate(cols, axis=1)
    return np.ascontiguousarray(full.T)[None].astype(np.float32)
